# version 02
# baseline (speedup 1.0000x reference)
import os
import numpy as np
from contextlib import ExitStack

import concourse.bass as bass
import concourse.mybir as mybir
from concourse.bass_utils import run_bass_kernel_spmd

F32 = mybir.dt.float32
BF16 = mybir.dt.bfloat16
I32 = mybir.dt.int32
AF = mybir.ActivationFunctionType
ALU = mybir.AluOpType
AX = mybir.AxisListType

D = 2048
S = 4096
H = 16
TOK = 2048
NKI = 16
FFH = 8192
EPS = 1e-5
INVD = 1.0 / D

ENGINES = ("tensor", "vector", "scalar", "gpsimd", "sync")
KDBG = os.environ.get("KDBG", "")


class Buf:
    __slots__ = ("name", "writer", "readers")
    def __init__(self, name):
        self.name = name
        self.writer = None
        self.readers = []


class Op:
    __slots__ = ("eng", "fn", "args", "kwargs", "idx", "is_dma", "prods",
                 "inc_marked", "cum")
    def __init__(self, eng, fn, args, kwargs, idx, is_dma):
        self.eng, self.fn, self.args, self.kwargs = eng, fn, args, kwargs
        self.idx, self.is_dma = idx, is_dma
        self.prods = []
        self.inc_marked = False
        self.cum = 0


class StopPlan(Exception):
    pass


class Plan:
    def __init__(self):
        self.ops = {e: [] for e in ENGINES}
        self.water = {}
        self.epoch = 0
        self.bufs = {}
        self.dead = False

    def buf(self, key):
        k = (key, self.epoch)
        if k not in self.bufs:
            self.bufs[k] = Buf(key)
        return self.bufs[k]

    def emit(self, eng, fn, *args, reads=(), writes=(), waits=(), **kwargs):
        if self.dead:
            return None
        is_dma = fn == "dma_start"
        op = Op(eng, fn, args, kwargs, len(self.ops[eng]), is_dma)
        prods = []
        for b in reads:
            if b.writer is not None:
                prods.append(b.writer)
        for b in writes:
            if b.writer is not None:
                prods.append(b.writer)
            prods.extend(b.readers)
        prods.extend(waits)
        best = {}
        for p in prods:
            if p is op:
                continue
            key = (p.eng, p.is_dma)
            if key not in best or p.idx > best[key].idx:
                best[key] = p
        for key, p in best.items():
            wkey = (eng, key)
            if self.water.get(wkey, -1) >= p.idx:
                continue
            self.water[wkey] = p.idx
            op.prods.append(p)
        self.ops[eng].append(op)
        for b in reads:
            b.readers.append(op)
        for b in writes:
            b.writer = op
            b.readers = []
        return op

    def last_real(self, eng):
        for op in reversed(self.ops[eng]):
            if op.fn != "_noop":
                return op
        return None

    def barrier(self):
        if self.dead:
            return
        lasts = []
        for e in ENGINES:
            last_c = last_d = None
            for op in reversed(self.ops[e]):
                if op.fn == "_noop":
                    continue
                if op.is_dma and last_d is None:
                    last_d = op
                if not op.is_dma and last_c is None:
                    last_c = op
                if last_c is not None and last_d is not None:
                    break
            if last_c is not None:
                lasts.append(last_c)
            if last_d is not None:
                lasts.append(last_d)
        for e in ENGINES:
            self.emit(e, "_noop", waits=lasts)
        self.epoch += 1

    def finalize(self):
        for e in ENGINES:
            for op in self.ops[e]:
                if op.is_dma:
                    op.inc_marked = True
                for p in op.prods:
                    p.inc_marked = True
        for e in ENGINES:
            cc = cd = 0
            for op in self.ops[e]:
                if op.inc_marked:
                    if op.is_dma:
                        cd += 16
                        op.cum = cd
                    else:
                        cc += 1
                        op.cum = cc

    def replay(self, engine, ename, sems):
        for op in self.ops[ename]:
            for p in op.prods:
                engine.wait_ge(sems[(p.eng, p.is_dma)], p.cum)
            if op.fn == "_noop":
                if op.inc_marked:
                    engine.nop().then_inc(sems[(op.eng, False)], 1)
                continue
            inst = getattr(engine, op.fn)(*op.args, **op.kwargs)
            if op.inc_marked:
                inst.then_inc(sems[(op.eng, op.is_dma)], 16 if op.is_dma else 1)


def build_nc():
    nc = bass.Bass("TRN2", target_bir_lowering=False, debug=False)
    P = Plan()
    em = P.emit
    B_ = P.buf
    dbg = {"outs": []}

    def inp(name, shape):
        return nc.dram_tensor(name, shape, F32, kind="ExternalInput").ap()

    xT = inp("xT", [D, S])
    xownT = inp("xownT", [D, TOK])
    wq_b = inp("wq_b", [H, 128, NKI * 128])
    wk_b = inp("wk_b", [H, 128, NKI * 128])
    wv_b = inp("wv_b", [H, 128, NKI * 128])
    wg_b = inp("wg_b", [NKI, 128, NKI * 128])
    wo_i = inp("wo", [D, D])
    vecs = inp("vecs", [128, 128])
    gammaP = inp("gammaP", [H, 1])
    wgv_b = inp("wgv_b", [64, 128, NKI * 256])
    wout_b = inp("wout_b", [16, 128, 64 * 128])
    hmask = inp("hmask", [128, 2])

    y = nc.dram_tensor("y", [NKI, 128, TOK], F32, kind="ExternalOutput").ap()

    d_ret = nc.dram_tensor("d_ret", [H, 128, S], F32).ap()
    d_gr = nc.dram_tensor("d_gr", [NKI, 128, TOK], BF16).ap()
    d_x2 = nc.dram_tensor("d_x2", [NKI, 128, TOK], F32).ap()
    d_wgvq = nc.dram_tensor("d_wgvq", [64, 128, NKI * 256], BF16).ap()
    d_woutq = nc.dram_tensor("d_woutq", [16, 128, 64 * 128], BF16).ap()
    d_u = nc.dram_tensor("d_u", [64, 128, TOK], BF16).ap()
    d_outf = nc.dram_tensor("d_outf", [NKI, 128, TOK], F32).ap()
    d_tiny = nc.dram_tensor("d_tiny", [8, 128], F32).ap()
    d_dmk = nc.dram_tensor("d_dmk", [H, 128, 2048], F32).ap()
    d_xf = nc.dram_tensor("d_xf", [H, 128, 512], F32).ap()
    d_rvt = nc.dram_tensor("d_rvt", [H, 128, 512], F32).ap()
    d_tiny2 = nc.dram_tensor("d_tiny2", [16, 1], F32).ap()

    with ExitStack() as stack:
        E = stack.enter_context
        off = [16640]

        def alloc(name, shape, dtype, at=None):
            nb = int(np.prod(shape[1:])) * (2 if dtype == BF16 else 4)
            if at is None:
                at = off[0]
                off[0] = at + ((nb + 31) & ~31)
            return nc.alloc_sbuf_tensor_at(name, shape, dtype, offset=at)

        eye_bf = alloc("eye_bf", [128, 128], BF16)
        eye_f = alloc("eye_f", [128, 128], F32)
        ones_bf = alloc("ones_bf", [128, 128], BF16)
        ones_f = alloc("ones_f", [128, 128], F32)
        onesrow = alloc("onesrow", [1, 128], F32)

        vecsT = alloc("vecsT", [128, 128], F32)
        hmask_s = alloc("hmask_s", [128, 2], F32)
        gam = alloc("gam", [H, 1], F32)
        chain = alloc("chain", [H, 12], F32)
        powrow = alloc("powrow", [H, 512], F32)
        ipowrow = alloc("ipowrow", [H, 512], F32)
        shiftrow = alloc("shiftrow", [H, 512], F32)
        rev_part = alloc("rev_part", [128, 4 * H], F32)
        fac512 = alloc("fac512", [128, H], F32)

        alpha_t = alloc("alpha_t", [128, H], F32)
        beta_t = alloc("beta_t", [128, H], F32)
        aa_t = alloc("aa_t", [128, H], F32)
        ab_t = alloc("ab_t", [128, H], F32)
        ssum = alloc("ssum", [128, H * 8], F32)
        ssq = alloc("ssq", [128, H * 8], F32)
        red2 = alloc("red2", [128, 32], F32)
        stsc = alloc("stsc", [128, 64], F32)
        S_pong = alloc("S_pong", [128, 128], F32)
        S_bf = alloc("S_bf", [128, 128], BF16)
        sgv_c = alloc("sgv_c", [128, 4], F32)
        sou_c = alloc("sou_c", [128, 4], F32)
        p1acc = alloc("p1acc", [128, 8], F32)
        c2_t = alloc("c2_t", [128, TOK], F32)
        d2_t = alloc("d2_t", [128, TOK], F32)

        tokmax = alloc("tokmax", [128, 8], F32)

        small_end = off[0]

        R0 = alloc("R0", [128, 32768], BF16)
        ra_at = off[0]
        RA = alloc("RA", [128, 16384], BF16)
        RAf = alloc("RAf", [128, 8192], F32, at=ra_at)
        RB = alloc("RB", [128, 8192], F32)
        RC = alloc("RC", [128, 8192], F32)
        rd_at = off[0]
        RD = alloc("RD", [128, 4096], F32)
        RDb = alloc("RDb", [128, 8192], BF16, at=rd_at)
        rc_at = None
        vecs_s = alloc("vecs_s", [128, 128], F32, at=rd_at + 512 * 4)
        rowsc_t = alloc("rowsc_t", [1, 2048], F32, at=rd_at + 2048 * 4)
        rowsc = rowsc_t
        i32scr = alloc("i32scr", [128, 2048], I32,
                       at=small_end + 65536 + 32768 + 32768 + 4096 * 4)
        scrow_t = alloc("scrow_t", [1, 1024], F32,
                        at=small_end + 65536 + 32768 + 32768 + 7168 * 4)
        scrow = scrow_t
        scrow2 = alloc("scrow2_t", [1, 1024], F32,
                       at=small_end + 65536 + 32768 + 32768 + 6144 * 4)

        print(f"[kernel] sbuf bytes/partition used: {off[0]} "
              f"(smalls {small_end})")

        def psal(name, shape, dtype):
            return nc.alloc_psum_tensor(name, shape, dtype)

        ps_a = psal("ps_a", [128, 512], F32)
        ps_b = psal("ps_b", [128, 512], F32)
        ps_c = psal("ps_c", [128, 512], F32)
        ps_d = psal("ps_d", [128, 512], F32)
        ps_e = psal("ps_e", [128, 1024], BF16)
        ps_f = psal("ps_f", [128, 512], F32)
        ps_g = psal("ps_g", [128, 128], F32)
        ps_h = psal("ps_h", [128, 512], F32)

        def mm_group(pst_ap, b_pst, pairs, read_bufs):
            n = len(pairs)
            o = None
            for i, (l, r) in enumerate(pairs):
                o = em("tensor", "matmul", out=pst_ap, lhsT=l, rhs=r,
                       reads=read_bufs, writes=[b_pst] if i == 0 else [],
                       start=(i == 0), stop=(i == n - 1))
            if o is not None:
                b_pst.writer = o
                b_pst.readers = []
            return o

        def tr_group(ps_ap_fn, b_pst, items, b_src, identity, b_eye):
            o = None
            for i, (oap, iap) in enumerate(items):
                o = em("tensor", "transpose", out=oap, in_=iap,
                       identity=identity, reads=[b_src, b_eye],
                       writes=[b_pst] if i == 0 else [])
            if o is not None:
                b_pst.writer = o
                b_pst.readers = []
            return o

        def dma(queue, out_ap, in_ap, reads=(), writes=(), **kw):
            return em(queue, "dma_start", out=out_ap, in_=in_ap,
                      reads=list(reads), writes=list(writes), **kw)

        def veccol(v, c):
            return vecsT[:, v * 16 + c: v * 16 + c + 1]

        tapn = [0]
        def tap(name, ap_, b_):
            if not os.environ.get("KTAP"):
                return
            sh = [int(xx) for xx in ap_.shape]
            o = nc.dram_tensor("tap_" + name, sh, ap_.dtype,
                               kind="ExternalOutput").ap()
            dma("sync", o, ap_, reads=[b_], writes=[B_("tap" + name)])
            dbg["taps"] = dbg.get("taps", []) + ["tap_" + name]

        def dump(phase, items):
            if KDBG != phase or P.dead:
                return
            last = None
            for name, ap_, b_ in items:
                sh = [int(xx) for xx in ap_.shape]
                o = nc.dram_tensor("dbg_" + name, sh, ap_.dtype,
                                   kind="ExternalOutput").ap()
                last = dma("sync", o, ap_, reads=[b_],
                           writes=[B_("dbg" + name)])
                dbg["outs"].append("dbg_" + name)
            em("sync", "_noop", waits=[last])
            P.dead = True


        b = B_
        em("gpsimd", "memset", eye_bf[:], 1.0, writes=[b("eye_bf")])
        em("gpsimd", "affine_select", out=eye_bf[:], in_=eye_bf[:],
           pattern=[[-1, 128]], base=0, channel_multiplier=1,
           compare_op=ALU.is_equal, fill=0.0, reads=[b("eye_bf")],
           writes=[b("eye_bf")])
        em("gpsimd", "memset", eye_f[:], 1.0, writes=[b("eye_f")])
        em("gpsimd", "affine_select", out=eye_f[:], in_=eye_f[:],
           pattern=[[-1, 128]], base=0, channel_multiplier=1,
           compare_op=ALU.is_equal, fill=0.0, reads=[b("eye_f")],
           writes=[b("eye_f")])
        em("gpsimd", "memset", ones_bf[:], 1.0, writes=[b("ones_bf")])
        em("gpsimd", "memset", ones_f[:], 1.0, writes=[b("ones_f")])
        em("gpsimd", "memset", onesrow[:], 1.0, writes=[b("onesrow")])

        dma("sync", vecs_s[:], vecs, writes=[b("vecs_s")])
        dma("sync", hmask_s[:], hmask, writes=[b("hmask")])
        dma("sync", gam[:], gammaP, writes=[b("gam")])

        em("tensor", "transpose", out=ps_f[:, 0:128], in_=vecs_s[:],
           identity=eye_f[:], reads=[b("vecs_s"), b("eye_f")],
           writes=[b("ps_f")])
        em("vector", "tensor_copy", out=vecsT[:], in_=ps_f[:, 0:128],
           reads=[b("ps_f")], writes=[b("vecsT")])

        em("vector", "tensor_copy", out=chain[:, 0:1], in_=gam[:],
           reads=[b("gam")], writes=[b("chain")])
        for t in range(1, 11):
            em("vector", "tensor_tensor", out=chain[:, t:t + 1],
               in0=chain[:, t - 1:t], in1=chain[:, t - 1:t], op=ALU.mult,
               reads=[b("chain")], writes=[b("chain")])
        em("vector", "memset", powrow[:, 0:1], 1.0, writes=[b("powrow")])
        seg = 1
        for t in range(9):
            em("vector", "tensor_scalar", out=powrow[:, seg:2 * seg],
               in0=powrow[:, 0:seg], scalar1=chain[:, t:t + 1], scalar2=None,
               op0=ALU.mult, reads=[b("powrow"), b("chain")],
               writes=[b("powrow")])
            seg *= 2
        em("vector", "reciprocal", out=ipowrow[:], in_=powrow[:],
           reads=[b("powrow")], writes=[b("ipow")])
        em("vector", "tensor_scalar", out=shiftrow[:], in0=powrow[:],
           scalar1=chain[:, 0:1], scalar2=None, op0=ALU.mult,
           reads=[b("powrow"), b("chain")], writes=[b("shift")])
        revrow = RD[:, 0:512]
        em("vector", "memset", revrow, 0.0, writes=[b("revrow")])
        em("vector", "tensor_scalar", out=revrow[0:16, :], in0=ipowrow[:],
           scalar1=powrow[:, 511:512], scalar2=None, op0=ALU.mult,
           reads=[b("ipow"), b("powrow")], writes=[b("revrow")])
        for s in range(4):
            em("tensor", "transpose", out=ps_f[:, 0:128],
               in_=revrow[:, s * 128:(s + 1) * 128], identity=eye_f[:],
               reads=[b("revrow"), b("eye_f")], writes=[b("ps_f")])
            em("vector", "tensor_copy", out=rev_part[:, s * 16:(s + 1) * 16],
               in_=ps_f[:, 0:16], reads=[b("ps_f")], writes=[b("rev_part")])
        dma("gpsimd", d_tiny2, chain[:, 9:10], reads=[b("chain")],
            writes=[b("d_tiny")])
        dma("gpsimd", rowsc[:, 0:16],
            d_tiny2.rearrange("p n -> n p"),
            reads=[b("d_tiny")], writes=[b("rowsc")],
            allow_slow_non_contiguous=True)
        mm_group(ps_h[:, 0:16], b("ps_h"),
                 [(onesrow[:, 0:128], rowsc[:, 0:16])],
                 [b("onesrow"), b("rowsc")])
        em("vector", "tensor_copy", out=fac512[:], in_=ps_h[:, 0:16],
           reads=[b("ps_h")], writes=[b("fac512")])

        dump("consts", [("powrow", powrow[:], b("powrow")),
                        ("revpart", rev_part[:], b("rev_part")),
                        ("fac512", fac512[:], b("fac512")),
                        ("vecsT", vecsT[:], b("vecsT")),
                        ("eyef", eye_f[:], b("eye_f")),
                        ("chain", chain[:], b("chain"))])
        P.barrier()

        dmk0 = RB[:, 0:2048]
        xf0 = RB[:, 2048:2560]
        rvt0 = RB[:, 2560:3072]
        for h in range(H):
            dma("gpsimd", rowsc[:, 0:512], powrow[h:h + 1, :],
                reads=[b("powrow")], writes=[b("rowsc")])
            dma("gpsimd", rowsc[:, 512:1024], ipowrow[h:h + 1, :],
                reads=[b("ipow")], writes=[b("rowsc")])
            dma("gpsimd", rowsc[:, 1024:1536], shiftrow[h:h + 1, :],
                reads=[b("shift")], writes=[b("rowsc")])
            for sm in range(4):
                mm_group(ps_h[:], b("ps_h"),
                         [(rowsc[:, 512 + sm * 128:512 + (sm + 1) * 128],
                           rowsc[:, 0:512])], [b("rowsc")])
                em("vector", "tensor_copy",
                   out=dmk0[:, sm * 512:(sm + 1) * 512], in_=ps_h[:],
                   reads=[b("ps_h")], writes=[b("dmk")])
                em("gpsimd", "affine_select",
                   out=dmk0[:, sm * 512:(sm + 1) * 512],
                   in_=dmk0[:, sm * 512:(sm + 1) * 512],
                   pattern=[[1, 512]], base=-128 * sm,
                   channel_multiplier=-1, compare_op=ALU.is_ge, fill=0.0,
                   reads=[b("dmk")], writes=[b("dmk")])
            mm_group(ps_h[:], b("ps_h"),
                     [(onesrow[:, 0:128], rowsc[:, 1024:1536])],
                     [b("onesrow"), b("rowsc")])
            em("vector", "tensor_copy", out=xf0[:], in_=ps_h[:],
               reads=[b("ps_h")], writes=[b("xf")])
            for sm in range(4):
                em("vector", "tensor_scalar",
                   out=rvt0[:, sm * 128:(sm + 1) * 128],
                   in0=ones_f[:, 0:128],
                   scalar1=rev_part[:, sm * 16 + h:sm * 16 + h + 1],
                   scalar2=None, op0=ALU.mult,
                   reads=[b("ones_f"), b("rev_part")], writes=[b("rvt")])
            dma("scalar", d_dmk[h], dmk0, reads=[b("dmk")], writes=[b("d_dmk")])
            dma("scalar", d_xf[h], xf0, reads=[b("xf")], writes=[b("d_xf")])
            dma("scalar", d_rvt[h], rvt0, reads=[b("rvt")], writes=[b("d_rvt")])
        P.barrier()

        a1 = RC[:, 4096:6144]
        b1 = RC[:, 6144:8192]

        def ln_stats_combine(a_ap, ba, b_ap, bb, sum_ap, bsum, sq_ap, bsq,
                             t1_ap, bt1, t2_ap, bt2):
            em("vector", "tensor_scalar", out=t1_ap, in0=sum_ap, scalar1=INVD,
               scalar2=None, op0=ALU.mult, reads=[bsum], writes=[bt1])
            em("vector", "tensor_scalar", out=t2_ap, in0=sq_ap, scalar1=INVD,
               scalar2=EPS, op0=ALU.mult, op1=ALU.add, reads=[bsq],
               writes=[bt2])
            em("gpsimd", "tensor_tensor", out=a_ap, in0=t1_ap, in1=t1_ap,
               op=ALU.mult, reads=[bt1], writes=[ba])
            em("vector", "tensor_tensor", out=b_ap, in0=t2_ap, in1=a_ap,
               op=ALU.subtract, reads=[bt2, ba], writes=[bb])
            em("scalar", "activation", out=t2_ap, in_=b_ap, func=AF.Sqrt,
               reads=[bb], writes=[bt2])
            em("vector", "reciprocal", out=a_ap, in_=t2_ap, reads=[bt2],
               writes=[ba])
            em("gpsimd", "tensor_tensor", out=t2_ap, in0=t1_ap, in1=a_ap,
               op=ALU.mult, reads=[bt1, ba], writes=[bt2])
            em("vector", "tensor_scalar", out=b_ap, in0=t2_ap, scalar1=-1.0,
               scalar2=None, op0=ALU.mult, reads=[bt2], writes=[bb])

        def ln_full(src_fn, dst_region, b_dst):
            lnS1 = RB[:, 0:2048]
            lnS2 = RB[:, 2048:4096]
            castt = RA[:, 0:1024]
            sqt = RA[:, 1024:2048]
            for sp in range(2):
                o1 = o2 = None
                for ki in range(NKI):
                    slot = ki % 2
                    xs = RD[:, slot * 2048:slot * 2048 + 1024]
                    bxs = b(f"rd{slot}")
                    dma("sync", xs, src_fn(ki, sp * 1024, 1024), writes=[bxs])
                    em("gpsimd", "tensor_copy", out=castt, in_=xs,
                       reads=[bxs], writes=[b("castt")])
                    em("scalar", "activation", out=sqt, in_=xs, func=AF.Square,
                       reads=[bxs], writes=[b("sqt")])
                    for sl in range(2):
                        kw = dict(start=(ki == 0), stop=(ki == NKI - 1))
                        psm, bpm = [(ps_a, b("ps_a")), (ps_b, b("ps_b"))][sl]
                        psq, bpq = [(ps_c, b("ps_c")), (ps_d, b("ps_d"))][sl]
                        o1 = em("tensor", "matmul", out=psm[:],
                                lhsT=ones_bf[:],
                                rhs=castt[:, sl * 512:(sl + 1) * 512],
                                reads=[b("ones_bf"), b("castt")],
                                writes=[bpm] if ki == 0 else [], **kw)
                        o2 = em("tensor", "matmul", out=psq[:],
                                lhsT=ones_bf[:],
                                rhs=sqt[:, sl * 512:(sl + 1) * 512],
                                reads=[b("ones_bf"), b("sqt")],
                                writes=[bpq] if ki == 0 else [], **kw)
                        if ki == NKI - 1 and o1 is not None:
                            bpm.writer = o1; bpm.readers = []
                            bpq.writer = o2; bpq.readers = []
                for sl in range(2):
                    w0 = sp * 1024 + sl * 512
                    psm, bpm = [(ps_a, b("ps_a")), (ps_b, b("ps_b"))][sl]
                    psq, bpq = [(ps_c, b("ps_c")), (ps_d, b("ps_d"))][sl]
                    em("vector", "tensor_copy", out=lnS1[:, w0:w0 + 512],
                       in_=psm[:], reads=[bpm], writes=[b("lnS1")])
                    em("vector", "tensor_copy", out=lnS2[:, w0:w0 + 512],
                       in_=psq[:], reads=[bpq], writes=[b("lnS2")])
            ln_stats_combine(a1, b("a1"), b1, b("b1"), lnS1, b("lnS1"),
                             lnS2, b("lnS2"), RC[:, 0:2048], b("rcs0"),
                             RC[:, 2048:4096], b("rcs1"))
            for ki in range(NKI):
                slot = ki % 2
                xs = RD[:, slot * 2048:slot * 2048 + 2048]
                bxs = b(f"rd{slot}")
                dma("sync", xs, src_fn(ki, 0, 2048), writes=[bxs])
                t1 = RC[:, slot * 2048: slot * 2048 + 2048]
                bt1 = b(f"rcs{slot}")
                em("vector", "tensor_tensor", out=t1, in0=xs, in1=a1,
                   op=ALU.mult, reads=[bxs, b("a1")], writes=[bt1])
                em("gpsimd", "tensor_tensor",
                   out=dst_region[:, ki * 2048:(ki + 1) * 2048],
                   in0=t1, in1=b1, op=ALU.add, reads=[bt1, b("b1")],
                   writes=[b_dst])

        qh = RA[:, 0:2048]
        kh = RA[:, 2048:4096]
        vst = RA[:, 4096:6144]
        vtm = RA[:, 6144:8192]
        vdec = RA[:, 8192:10240]
        ktm = RA[:, 10240:12288]
        stm4 = RA[:, 12288:14336]
        retb = RB[:, 0:2048]
        dmk = RB[:, 2048:4096]
        S_all = RB[:, 4096:6144]
        xf_t = RB[:, 6144:6656]
        revtile = RB[:, 6656:7168]
        crossm = RB[:, 7168:7680]
        sqscr = RB[:, 7680:8192]
        ln1res = R0[:, 0:NKI * 2048]

        def proj_head(wbf_ap, b_wbf, src_region, b_src, dst_ap, b_dst,
                      bias_col, act):
            for sl in range(4):
                pst, bp = [(ps_a, b("ps_a")), (ps_b, b("ps_b"))][sl % 2]
                pairs = [(wbf_ap[:, ki * 128:(ki + 1) * 128],
                          src_region[:, ki * 2048 + sl * 512:
                                      ki * 2048 + sl * 512 + 512])
                         for ki in range(NKI)]
                mm_group(pst[:], bp, pairs, [b_wbf, b_src])
                em("scalar", "activation",
                   out=dst_ap[:, sl * 512:(sl + 1) * 512], in_=pst[:],
                   func=act, bias=bias_col, scale=1.0,
                   reads=[bp, b("vecsT")], writes=[b_dst])

        def attn_half(hs):
            wsl = [RC[:, 0:2048], RC[:, 2048:4096]]
            wbf = [RDb[:, 0:2048], RDb[:, 2048:4096]]
            wmats = [wq_b, wk_b, wv_b]
            def wload(idx, queue="sync"):
                mat, hh, slot = wmats[idx % 3], idx // 3, idx % 2
                dma(queue, wsl[slot], mat[hh], writes=[b(f"ws{slot}")])
                em("scalar", "activation", out=wbf[slot], in_=wsl[slot],
                   func=AF.Copy, reads=[b(f"ws{slot}")],
                   writes=[b(f"wb{slot}")])
            wload(0)
            for h in range(H):
                dma("sync", dmk[:], d_dmk[h], reads=[b("d_dmk")],
                    writes=[b("dmk")])
                dma("sync", xf_t[:], d_xf[h], reads=[b("d_xf")],
                    writes=[b("xf")])
                dma("sync", revtile[:], d_rvt[h], reads=[b("d_rvt")],
                    writes=[b("rvt")])

                wload(h * 3 + 1)
                proj_head(wbf[(h * 3) % 2], b(f"wb{(h * 3) % 2}"), ln1res,
                          b("ln1res"), qh, b("qh"), veccol(0, h), AF.Identity)
                wload(h * 3 + 2)
                proj_head(wbf[(h * 3 + 1) % 2], b(f"wb{(h * 3 + 1) % 2}"),
                          ln1res, b("ln1res"), kh, b("kh"), veccol(1, h),
                          AF.Identity)
                if h < H - 1:
                    wload(h * 3 + 3)
                proj_head(wbf[(h * 3 + 2) % 2], b(f"wb{(h * 3 + 2) % 2}"),
                          ln1res, b("ln1res"), vst, b("vst"), veccol(2, h),
                          AF.Identity)

                for grp in range(4):
                    items = [(ps_e[:, j * 128:(j + 1) * 128],
                              kh[:, (grp * 4 + j) * 128:(grp * 4 + j + 1) * 128])
                             for j in range(4)]
                    tr_group(None, b("ps_e"), items, b("kh"), eye_bf[:],
                             b("eye_bf"))
                    em("vector", "tensor_copy",
                       out=ktm[:, grp * 512:(grp + 1) * 512],
                       in_=ps_e[:, 0:512], reads=[b("ps_e")], writes=[b("ktm")])
                    items = [(ps_e[:, 512 + j * 128:512 + (j + 1) * 128],
                              vst[:, (grp * 4 + j) * 128:(grp * 4 + j + 1) * 128])
                             for j in range(4)]
                    tr_group(None, b("ps_e"), items, b("vst"), eye_bf[:],
                             b("eye_bf"))
                    em("vector", "tensor_copy",
                       out=vtm[:, grp * 512:(grp + 1) * 512],
                       in_=ps_e[:, 512:1024], reads=[b("ps_e")],
                       writes=[b("vtm")])
                    em("gpsimd", "tensor_tensor",
                       out=vdec[:, grp * 512:(grp + 1) * 512],
                       in0=vtm[:, grp * 512:(grp + 1) * 512], in1=revtile[:],
                       op=ALU.mult, reads=[b("vtm"), b("rvt")],
                       writes=[b("vdec")])

                for t in range(4):
                    gt = hs * 4 + t
                    q_sl = qh[:, t * 512:(t + 1) * 512]
                    for s in range(4):
                        pst, bp = [(ps_a, b("ps_a")), (ps_b, b("ps_b"))][s % 2]
                        mm_group(pst[:], bp,
                                 [(kh[:, t * 512 + s * 128:
                                       t * 512 + (s + 1) * 128], q_sl)],
                                 [b("kh"), b("qh")])
                        em("vector", "tensor_tensor",
                           out=stm4[:, s * 512:(s + 1) * 512], in0=pst[:],
                           in1=dmk[:, s * 512:(s + 1) * 512], op=ALU.mult,
                           reads=[bp, b("dmk")], writes=[b("stm")])
                    pairs = [(vtm[:, t * 512 + s * 128:t * 512 + (s + 1) * 128],
                              stm4[:, s * 512:(s + 1) * 512]) for s in range(4)]
                    mm_group(ps_c[:], b("ps_c"), pairs, [b("vtm"), b("stm")])
                    if gt > 0:
                        em("vector", "tensor_copy", out=S_bf[:],
                           in_=S_all[:, h * 128:(h + 1) * 128],
                           reads=[b("S_all")], writes=[b("S_bf")])
                        mm_group(ps_d[:], b("ps_d"), [(S_bf[:], q_sl)],
                                 [b("S_bf"), b("qh")])
                        em("vector", "tensor_tensor", out=crossm[:],
                           in0=ps_d[:], in1=xf_t[:], op=ALU.mult,
                           reads=[b("ps_d"), b("xf")], writes=[b("crm")])
                        em("vector", "scalar_tensor_tensor",
                           out=retb[:, t * 512:(t + 1) * 512], in0=ps_c[:],
                           scalar=1.0, in1=crossm[:], op0=ALU.mult,
                           op1=ALU.add,
                           accum_out=ssum[:, h * 8 + gt:h * 8 + gt + 1],
                           reads=[b("ps_c"), b("crm")],
                           writes=[b("ret"), b("ssum")])
                    else:
                        em("vector", "tensor_scalar",
                           out=retb[:, t * 512:(t + 1) * 512], in0=ps_c[:],
                           scalar1=1.0, scalar2=0.0, op0=ALU.mult, op1=ALU.add,
                           accum_out=ssum[:, h * 8 + gt:h * 8 + gt + 1],
                           reads=[b("ps_c")], writes=[b("ret"), b("ssum")])
                    em("scalar", "activation", out=sqscr[:],
                       in_=retb[:, t * 512:(t + 1) * 512], func=AF.Square,
                       accum_out=ssq[:, h * 8 + gt:h * 8 + gt + 1],
                       reads=[b("ret")], writes=[b("sqscr"), b("ssq")])
                    pairs = [(ktm[:, t * 512 + s * 128:t * 512 + (s + 1) * 128],
                              vdec[:, t * 512 + s * 128:t * 512 + (s + 1) * 128])
                             for s in range(4)]
                    mm_group(ps_g[:], b("ps_g"), pairs, [b("ktm"), b("vdec")])
                    em("vector", "scalar_tensor_tensor", out=S_pong[:],
                       in0=S_all[:, h * 128:(h + 1) * 128],
                       scalar=fac512[:, h:h + 1], in1=ps_g[:], op0=ALU.mult,
                       op1=ALU.add,
                       reads=[b("S_all"), b("fac512"), b("ps_g")],
                       writes=[b("S_pong")])
                    em("gpsimd", "tensor_copy",
                       out=S_all[:, h * 128:(h + 1) * 128], in_=S_pong[:],
                       reads=[b("S_pong")], writes=[b("S_all")])
                dma("scalar", d_ret[h][:, hs * 2048:(hs + 1) * 2048], retb[:],
                    reads=[b("ret")], writes=[b("d_ret")])

        em("vector", "memset", S_all[:], 0.0, writes=[b("S_all")])
        em("vector", "memset", ssum[:], 0.0, writes=[b("ssum")])
        em("vector", "memset", ssq[:], 0.0, writes=[b("ssq")])

        def xsrc_half(hs):
            def fn(ki, toff, w):
                t0 = hs * 2048 + toff
                return xT[ki * 128:(ki + 1) * 128, t0:t0 + w]
            return fn

        ln_full(xsrc_half(0), ln1res, b("ln1res"))
        tap("ln1h0", R0[:, 0:512], b("ln1res"))
        tap("a1h0", a1[:, 0:512], b("a1"))
        dump("ln1", [("ln1res", R0[:, 0:4096], b("ln1res")),
                     ("a1", a1, b("a1")), ("b1", b1, b("b1"))])
        P.barrier()
        attn_half(0)
        P.barrier()
        dump("reta", [("ssum", ssum[:], b("ssum")),
                      ("ssq", ssq[:], b("ssq")),
                      ("S_all", S_all[:, 0:256], b("S_all"))])
        ln_full(xsrc_half(1), ln1res, b("ln1res"))
        tap("ln1h1", R0[:, 0:512], b("ln1res"))
        P.barrier()
        attn_half(1)
        tap("ssum", ssum[:], b("ssum"))
        tap("ssq", ssq[:], b("ssq"))
        tap("ret5", d_ret[5], b("d_ret"))
        P.barrier()
        dump("ret", [("ssum", ssum[:], b("ssum")), ("ssq", ssq[:], b("ssq"))])
        dump("retval", [("ret0", d_ret[0], b("d_ret")),
                        ("ret7", d_ret[7], b("d_ret"))])

        for h in range(H):
            em("vector", "tensor_reduce", out=red2[:, 2 * h:2 * h + 1],
               in_=ssum[:, h * 8:(h + 1) * 8], axis=AX.X, op=ALU.add,
               reads=[b("ssum")], writes=[b("red2")])
            em("vector", "tensor_reduce", out=red2[:, 2 * h + 1:2 * h + 2],
               in_=ssq[:, h * 8:(h + 1) * 8], axis=AX.X, op=ALU.add,
               reads=[b("ssq")], writes=[b("red2")])
        mm_group(ps_h[:, 0:32], b("ps_h"), [(ones_f[:], red2[:])],
                 [b("ones_f"), b("red2")])
        NTOT = float(S * 128)
        em("vector", "tensor_scalar", out=stsc[:, 0:16], in0=ps_h[:, 0:32:2],
           scalar1=1.0 / NTOT, scalar2=None, op0=ALU.mult, reads=[b("ps_h")],
           writes=[b("stsc")])
        em("vector", "tensor_scalar", out=stsc[:, 16:32], in0=ps_h[:, 1:32:2],
           scalar1=1.0 / NTOT, scalar2=None, op0=ALU.mult, reads=[b("ps_h")],
           writes=[b("stsc")])
        em("gpsimd", "tensor_tensor", out=stsc[:, 32:48], in0=stsc[:, 0:16],
           in1=stsc[:, 0:16], op=ALU.mult, reads=[b("stsc")],
           writes=[b("stsc")])
        em("vector", "tensor_tensor", out=stsc[:, 48:64], in0=stsc[:, 16:32],
           in1=stsc[:, 32:48], op=ALU.subtract, reads=[b("stsc")],
           writes=[b("stsc")])
        em("vector", "tensor_scalar", out=stsc[:, 32:48], in0=stsc[:, 48:64],
           scalar1=EPS, scalar2=None, op0=ALU.add, reads=[b("stsc")],
           writes=[b("stsc")])
        em("scalar", "activation", out=stsc[:, 48:64], in_=stsc[:, 32:48],
           func=AF.Sqrt, reads=[b("stsc")], writes=[b("stsc")])
        em("vector", "reciprocal", out=stsc[:, 32:48], in_=stsc[:, 48:64],
           reads=[b("stsc")], writes=[b("stsc")])
        em("vector", "tensor_tensor", out=alpha_t[:], in0=stsc[:, 32:48],
           in1=vecsT[:, 5 * 16:6 * 16], op=ALU.mult,
           reads=[b("stsc"), b("vecsT")], writes=[b("alpha")])
        em("gpsimd", "tensor_tensor", out=stsc[:, 48:64], in0=stsc[:, 0:16],
           in1=alpha_t[:], op=ALU.mult, reads=[b("stsc"), b("alpha")],
           writes=[b("stsc")])
        em("vector", "tensor_tensor", out=beta_t[:],
           in0=vecsT[:, 6 * 16:7 * 16], in1=stsc[:, 48:64], op=ALU.subtract,
           reads=[b("vecsT"), b("stsc")], writes=[b("beta")])
        em("vector", "tensor_scalar", out=aa_t[:], in0=alpha_t[:],
           scalar1=hmask_s[:, 0:1], scalar2=None, op0=ALU.mult,
           reads=[b("alpha"), b("hmask")], writes=[b("aa")])
        em("vector", "tensor_scalar", out=ab_t[:], in0=alpha_t[:],
           scalar1=hmask_s[:, 1:2], scalar2=None, op0=ALU.mult,
           reads=[b("alpha"), b("hmask")], writes=[b("ab")])
        tap("alpha", alpha_t[:], b("alpha"))
        tap("beta", beta_t[:], b("beta"))
        P.barrier()
        dump("gn", [("alpha", alpha_t[:], b("alpha")),
                    ("beta", beta_t[:], b("beta"))])

        ln1own = R0[:, 0:NKI * 2048]

        def xown_src(ki, toff, w):
            return xownT[ki * 128:(ki + 1) * 128, toff:toff + w]

        ln_full(xown_src, ln1own, b("ln1own"))
        tap("lnown", R0[:, 0:512], b("ln1own"))
        P.barrier()
        dump("lnown", [("ln1own", R0[:, 0:4096], b("ln1own"))])

        gatec = RA[:, 0:2048]
        roc = RA[:, 2048:4096]
        grc = RA[:, 4096:6144]
        wbf0 = RDb[:, 0:2048]
        reta_s = RB[:, 0:2048]
        retb_s = RB[:, 2048:4096]
        rosc = RB[:, 4096:6144]
        for mo in range(NKI):
            dma("sync", RC[:, 0:2048], wg_b[mo], writes=[b("ws0")])
            em("gpsimd", "tensor_copy", out=wbf0, in_=RC[:, 0:2048],
               reads=[b("ws0")], writes=[b("wb0")])
            proj_head(wbf0, b("wb0"), ln1own, b("ln1own"), gatec, b("gate"),
                      veccol(3, mo), AF.Silu)
            dma("sync", reta_s, d_ret[mo][:, 0:2048], reads=[b("d_ret")],
                writes=[b("reta")])
            dma("sync", retb_s, d_ret[mo][:, 2048:4096], reads=[b("d_ret")],
                writes=[b("retb2")])
            em("vector", "tensor_scalar", out=rosc, in0=reta_s,
               scalar1=aa_t[:, mo:mo + 1], scalar2=beta_t[:, mo:mo + 1],
               op0=ALU.mult, op1=ALU.add,
               reads=[b("reta"), b("aa"), b("beta")], writes=[b("rosc")])
            em("vector", "scalar_tensor_tensor", out=roc, in0=retb_s,
               scalar=ab_t[:, mo:mo + 1], in1=rosc, op0=ALU.mult, op1=ALU.add,
               reads=[b("retb2"), b("ab"), b("rosc")], writes=[b("roc")])
            em("gpsimd", "tensor_tensor", out=grc, in0=gatec, in1=roc,
               op=ALU.mult, reads=[b("gate"), b("roc")], writes=[b("grc")])
            dma("scalar", d_gr[mo], grc, reads=[b("grc")], writes=[b("d_gr")])
        P.barrier()
        dump("gr", [("gr15", grc, b("grc"))])

        wo_res = R0[:, 0:NKI * 2048]
        for ki in range(NKI):
            slot = ki % 2
            ws = RC[:, slot * 2048:(slot + 1) * 2048]
            dma("sync", ws, wo_i[ki * 128:(ki + 1) * 128, :],
                writes=[b(f"ws{slot}")])
            em("gpsimd", "tensor_copy",
               out=wo_res[:, ki * 2048:(ki + 1) * 2048], in_=ws,
               reads=[b(f"ws{slot}")], writes=[b("wo_res")])

        grslab = RA[:, 0:NKI * 512]
        cast3 = RA[:, NKI * 512:NKI * 512 + 512]
        sq3 = RA[:, NKI * 512 + 512:NKI * 512 + 1024]
        a3 = RC[:, 4096:4608]
        b3 = RC[:, 4608:5120]
        l3s = RC[:, 5120:5632]
        l3q = RC[:, 5632:6144]
        x2c = RC[:, 6144:6656]
        t3a = RC[:, 6656:7168]
        t3b = RC[:, 7168:7680]
        t3c = RC[:, 7680:8192]
        for sl in range(4):
            for ki in range(NKI):
                dma("sync", grslab[:, ki * 512:(ki + 1) * 512],
                    d_gr[ki][:, sl * 512:(sl + 1) * 512], reads=[b("d_gr")],
                    writes=[b("grsl")])
            o1 = o2 = None
            for mo in range(NKI):
                pst, bp = [(ps_a, b("ps_a")), (ps_b, b("ps_b"))][mo % 2]
                pairs = [(wo_res[:, ki * 2048 + mo * 128:
                                 ki * 2048 + mo * 128 + 128],
                          grslab[:, ki * 512:(ki + 1) * 512])
                         for ki in range(NKI)]
                mm_group(pst[:], bp, pairs, [b("wo_res"), b("grsl")])
                em("scalar", "activation",
                   out=RB[:, mo * 512:(mo + 1) * 512], in_=pst[:],
                   func=AF.Identity, bias=veccol(4, mo), scale=1.0,
                   reads=[bp, b("vecsT")], writes=[b("ao")])
                em("gpsimd", "tensor_copy", out=cast3,
                   in_=RB[:, mo * 512:(mo + 1) * 512], reads=[b("ao")],
                   writes=[b("cast3")])
                em("scalar", "activation", out=sq3,
                   in_=RB[:, mo * 512:(mo + 1) * 512], func=AF.Square,
                   reads=[b("ao")], writes=[b("sq3")])
                kw = dict(start=(mo == 0), stop=(mo == NKI - 1))
                o1 = em("tensor", "matmul", out=ps_c[:], lhsT=ones_bf[:],
                        rhs=cast3, reads=[b("ones_bf"), b("cast3")],
                        writes=[b("ps_c")] if mo == 0 else [], **kw)
                o2 = em("tensor", "matmul", out=ps_d[:], lhsT=ones_bf[:],
                        rhs=sq3, reads=[b("ones_bf"), b("sq3")],
                        writes=[b("ps_d")] if mo == 0 else [], **kw)
            if o1 is not None:
                if o1 is not None:
                    b("ps_c").writer = o1; b("ps_c").readers = []
                    b("ps_d").writer = o2; b("ps_d").readers = []
            em("vector", "tensor_copy", out=l3s, in_=ps_c[:],
               reads=[b("ps_c")], writes=[b("l3s")])
            em("vector", "tensor_copy", out=l3q, in_=ps_d[:],
               reads=[b("ps_d")], writes=[b("l3q")])
            ln_stats_combine(a3, b("a3"), b3, b("b3"), l3s, b("l3s"),
                             l3q, b("l3q"), t3a, b("t3a"), t3b, b("t3b"))
            o1 = o2 = None
            for ki in range(NKI):
                slot = ki % 2
                xs = RD[:, slot * 2048:slot * 2048 + 512]
                bxs = b(f"rd{slot}")
                dma("sync", xs, xownT[ki * 128:(ki + 1) * 128,
                                      sl * 512:(sl + 1) * 512], writes=[bxs])
                em("vector", "tensor_tensor", out=t3a,
                   in0=RB[:, ki * 512:(ki + 1) * 512], in1=a3, op=ALU.mult,
                   reads=[b("ao"), b("a3")], writes=[b("t3a")])
                em("gpsimd", "tensor_tensor", out=t3b, in0=t3a, in1=b3,
                   op=ALU.add, reads=[b("t3a"), b("b3")], writes=[b("t3b")])
                em("vector", "tensor_tensor", out=x2c, in0=t3b, in1=xs,
                   op=ALU.add, reads=[b("t3b"), bxs], writes=[b("x2c")])
                dma("scalar", d_x2[ki][:, sl * 512:(sl + 1) * 512], x2c,
                    reads=[b("x2c")], writes=[b("d_x2")])
                em("gpsimd", "tensor_copy", out=cast3, in_=x2c,
                   reads=[b("x2c")], writes=[b("cast3")])
                em("scalar", "activation", out=sq3, in_=x2c, func=AF.Square,
                   reads=[b("x2c")], writes=[b("sq3")])
                kw = dict(start=(ki == 0), stop=(ki == NKI - 1))
                o1 = em("tensor", "matmul", out=ps_f[:], lhsT=ones_bf[:],
                        rhs=cast3, reads=[b("ones_bf"), b("cast3")],
                        writes=[b("ps_f")] if ki == 0 else [], **kw)
                o2 = em("tensor", "matmul", out=ps_h[:], lhsT=ones_bf[:],
                        rhs=sq3, reads=[b("ones_bf"), b("sq3")],
                        writes=[b("ps_h")] if ki == 0 else [], **kw)
            if o1 is not None:
                if o1 is not None:
                    b("ps_f").writer = o1; b("ps_f").readers = []
                    b("ps_h").writer = o2; b("ps_h").readers = []
            em("vector", "tensor_copy", out=l3s, in_=ps_f[:],
               reads=[b("ps_f")], writes=[b("l3s")])
            em("vector", "tensor_copy", out=l3q, in_=ps_h[:],
               reads=[b("ps_h")], writes=[b("l3q")])
            ln_stats_combine(a3, b("a3"), b3, b("b3"), l3s, b("l3s"),
                             l3q, b("l3q"), t3a, b("t3a"), t3b, b("t3b"))
            em("gpsimd", "tensor_tensor", out=t3a, in0=a3, in1=a3,
               op=ALU.mult, reads=[b("a3")], writes=[b("t3a")])
            em("vector", "tensor_scalar", out=t3b, in0=t3a, scalar1=-EPS,
               scalar2=1.0 + EPS, op0=ALU.mult, op1=ALU.add,
               reads=[b("t3a")], writes=[b("t3b")])
            em("scalar", "activation", out=t3c, in_=t3b, func=AF.Sqrt,
               reads=[b("t3b")], writes=[b("t3c")])
            em("vector", "reciprocal", out=t3a, in_=t3c, reads=[b("t3c")],
               writes=[b("t3a")])
            em("vector", "tensor_tensor",
               out=c2_t[:, sl * 512:(sl + 1) * 512], in0=a3, in1=t3a,
               op=ALU.mult, reads=[b("a3"), b("t3a")], writes=[b("c2")])
            em("gpsimd", "tensor_tensor",
               out=d2_t[:, sl * 512:(sl + 1) * 512], in0=b3, in1=t3a,
               op=ALU.mult, reads=[b("b3"), b("t3a")], writes=[b("d2")])
        tap("retall", d_ret, b("d_ret"))
        tap("grall", d_gr, b("d_gr"))
        tap("c2", c2_t[:], b("c2"))
        tap("d2", d2_t[:], b("d2"))
        tap("x2s0", d_x2[0], b("d_x2"))
        tap("gr5", d_gr[5], b("d_gr"))
        P.barrier()
        dump("x2", [("c2", c2_t[:], b("c2")), ("d2", d2_t[:], b("d2")),
                    ("x2f", d_x2, b("d_x2")), ("grf", d_gr, b("d_gr")),
                    ("alpha", alpha_t[:], b("alpha")),
                    ("beta", beta_t[:], b("beta"))])

        def absum(nblk, blk_ap, subw, outc):
            em("gpsimd", "memset", p1acc[:, outc:outc + 1], 0.0,
               writes=[b("p1")])
            for blk in range(nblk):
                ap_ = blk_ap(blk)
                nsub = subw // 2048
                for sub in range(nsub):
                    slot = (blk * nsub + sub) % 2
                    xs = RD[:, slot * 2048:(slot + 1) * 2048]
                    dma("sync", xs,
                        ap_[:, sub * 2048:(sub + 1) * 2048],
                        writes=[b(f"rd{slot}")])
                    em("vector", "tensor_reduce", out=p1acc[:, 2:3],
                       in_=xs, axis=AX.X, op=ALU.add,
                       apply_absolute_value=True,
                       reads=[b(f"rd{slot}")], writes=[b("p1")])
                    em("gpsimd", "tensor_tensor", out=p1acc[:, 4:5],
                       in0=p1acc[:, outc:outc + 1], in1=p1acc[:, 2:3],
                       op=ALU.add, reads=[b("p1")], writes=[b("p1")])
                    em("gpsimd", "tensor_copy", out=p1acc[:, outc:outc + 1],
                       in_=p1acc[:, 4:5], reads=[b("p1")], writes=[b("p1")])

        absum(64, lambda k: wgv_b[k], 4096, 0)
        absum(16, lambda k: wout_b[k], 8192, 1)
        mm_group(ps_h[:, 0:2], b("ps_h"), [(ones_f[:], p1acc[:, 0:2])],
                 [b("ones_f"), b("p1")])
        em("vector", "tensor_scalar", out=sgv_c[:, 1:2], in0=ps_h[:, 0:1],
           scalar1=1.0 / (16384.0 * 2048.0), scalar2=EPS, op0=ALU.mult,
           op1=ALU.max, reads=[b("ps_h")], writes=[b("sgv")])
        em("vector", "reciprocal", out=sgv_c[:, 0:1], in_=sgv_c[:, 1:2],
           reads=[b("sgv")], writes=[b("sgv")])
        em("vector", "tensor_scalar", out=sou_c[:, 1:2], in0=ps_h[:, 1:2],
           scalar1=1.0 / (2048.0 * 8192.0), scalar2=EPS, op0=ALU.mult,
           op1=ALU.max, reads=[b("ps_h")], writes=[b("sou")])
        em("vector", "reciprocal", out=sou_c[:, 0:1], in_=sou_c[:, 1:2],
           reads=[b("sou")], writes=[b("sou")])
        tap("sgv", sgv_c[:], b("sgv"))
        tap("sou", sou_c[:], b("sou"))
        P.barrier()
        dump("p1", [("sgv", sgv_c[:], b("sgv")), ("sou", sou_c[:], b("sou"))])

        umx = RB[:, 0:1024]
        umn = RB[:, 1024:2048]
        scu = RB[:, 2048:3072]
        invu = RB[:, 3072:4096]
        fq1 = RB[:, 4096:5120]
        fu1 = RB[:, 5120:6144]
        fu2 = RB[:, 6144:7168]
        dscT = RB[:, 7168:8192]
        hqT = R0[:, 0:NKI * 1024]
        uslot = RA[:, 0:2048]
        wQ = RC[:, 0:4096]
        wQb = RA[:, 2048:6144]

        def tchain(mxa, bmx, mna, bmn, sc_ap, bsc, inv_ap, binv):
            em("vector", "tensor_scalar", out=fu1, in0=mna, scalar1=-1.0,
               scalar2=None, op0=ALU.mult, reads=[bmn], writes=[b("fu1")])
            em("vector", "tensor_tensor", out=fu2, in0=mxa, in1=fu1,
               op=ALU.max, reads=[bmx, b("fu1")], writes=[b("fu2")])
            em("gpsimd", "tensor_reduce", out=scrow[:, 0:1024], in_=fu2,
               axis=AX.C, op=ALU.max, reads=[b("fu2")], writes=[b("scrow")])
            em("vector", "tensor_scalar", out=scrow2[:, 0:1024],
               in0=scrow[:, 0:1024], scalar1=EPS, scalar2=None, op0=ALU.max,
               reads=[b("scrow")], writes=[b("scrow2")])
            for sb2 in range(2):
                mm_group(ps_h[:], b("ps_h"),
                         [(onesrow[:, 0:128],
                           scrow2[:, sb2 * 512:(sb2 + 1) * 512])],
                         [b("onesrow"), b("scrow2")])
                em("vector", "tensor_scalar",
                   out=inv_ap[:, sb2 * 512:(sb2 + 1) * 512], in0=ps_h[:],
                   scalar1=1.0 / 127.0, scalar2=None, op0=ALU.mult,
                   reads=[b("ps_h")], writes=[binv])
                em("vector", "reciprocal",
                   out=sc_ap[:, sb2 * 512:(sb2 + 1) * 512],
                   in_=inv_ap[:, sb2 * 512:(sb2 + 1) * 512],
                   reads=[binv], writes=[bsc])

        for g in range(2):
            gbase = g * 1024

            def h_chunk(ki, dst_ap, b_dst):
                slot = ki % 2
                xs = RD[:, slot * 2048:slot * 2048 + 1024]
                bxs = b(f"rd{slot}")
                dma("sync", xs, d_x2[ki][:, gbase:gbase + 1024],
                    reads=[b("d_x2")], writes=[bxs])
                em("vector", "tensor_tensor", out=fq1, in0=xs,
                   in1=c2_t[:, gbase:gbase + 1024], op=ALU.mult,
                   reads=[bxs, b("c2")], writes=[b("fq1")])
                em("gpsimd", "tensor_tensor", out=fu1, in0=fq1,
                   in1=d2_t[:, gbase:gbase + 1024], op=ALU.add,
                   reads=[b("fq1"), b("d2")], writes=[b("fu1")])
                em("vector", "tensor_scalar", out=dst_ap, in0=fu1,
                   scalar1=veccol(7, ki), scalar2=None, op0=ALU.mult,
                   reads=[b("fu1"), b("vecsT")], writes=[b_dst])

            em("vector", "memset", umx[:], -1e30, writes=[b("umx")])
            em("vector", "memset", umn[:], 1e30, writes=[b("umn")])
            for ki in range(NKI):
                h_chunk(ki, fu2, b("fu2"))
                src, bs = (umx, b("umx")) if ki % 2 == 0 else (scu, b("scu"))
                dst, bd = (scu, b("scu")) if ki % 2 == 0 else (umx, b("umx"))
                em("vector", "tensor_tensor", out=dst, in0=src, in1=fu2,
                   op=ALU.max, reads=[bs, b("fu2")], writes=[bd])
                src, bs = (umn, b("umn")) if ki % 2 == 0 else (invu, b("invu"))
                dst, bd = (invu, b("invu")) if ki % 2 == 0 else (umn, b("umn"))
                em("vector", "tensor_tensor", out=dst, in0=src, in1=fu2,
                   op=ALU.min, reads=[bs, b("fu2")], writes=[bd])
            tchain(umx, b("umx"), umn, b("umn"), scu, b("scu"), invu,
                   b("invu"))
            em("vector", "tensor_scalar", out=dscT, in0=invu,
               scalar1=sgv_c[:, 1:2], scalar2=None, op0=ALU.mult,
               reads=[b("invu"), b("sgv")], writes=[b("dsc")])
            tap(f"scuh{g}", scu[:], b("scu"))
            for ki in range(NKI):
                h_chunk(ki, fu2, b("fu2"))
                em("vector", "tensor_tensor", out=fq1, in0=fu2, in1=scu,
                   op=ALU.mult, reads=[b("fu2"), b("scu")], writes=[b("fq1")])
                em("gpsimd", "tensor_copy", out=i32scr[:, 0:1024], in_=fq1,
                   reads=[b("fq1")], writes=[b("i32")])
                em("vector", "tensor_scalar",
                   out=hqT[:, ki * 1024:(ki + 1) * 1024],
                   in0=i32scr[:, 0:1024],
                   scalar1=-128.0, scalar2=127.0, op0=ALU.max, op1=ALU.min,
                   reads=[b("i32")], writes=[b("hq")])
            dump("hq" + str(g), [("hq", R0[:, 0:4096], b("hq")),
                                 ("scu", scu, b("scu")),
                                 ("dsc", dscT, b("dsc")),
                                 ("x2d", d_x2, b("d_x2")),
                                 ("c2", c2_t[:], b("c2")),
                                 ("d2", d2_t[:], b("d2")),
                                 ("sgv", sgv_c[:], b("sgv"))])

            em("vector", "tensor_copy", out=RA[:, 10240:11264],
               in_=dscT[:, 0:1024], reads=[b("dsc")], writes=[b("dsccp")])
            em("vector", "memset", umx[:], 0.0, writes=[b("umx2")])
            em("vector", "memset", umn[:], 0.0, writes=[b("umn2")])
            wQs = [RC[:, 0:4096], RD[:, 0:4096]]
            wQbs = [RA[:, 2048:6144], RA[:, 6144:10240]]

            def wgv_stage(c):
                slot = c % 2
                if g == 0:
                    extra = [b("rd0"), b("rd1")] if slot == 1 else [b("rcs0"),
                                                                    b("rcs1")]
                    dma("sync", wQs[slot], wgv_b[c],
                        writes=[b(f"wQ{slot}")] + extra)
                    for qs in range(2):
                        em("scalar", "activation", out=i32scr[:],
                           in_=wQs[slot][:, qs * 2048:(qs + 1) * 2048],
                           func=AF.Identity, scale=sgv_c[:, 0:1], bias=0.0,
                           reads=[b(f"wQ{slot}"), b("sgv")], writes=[b("i32")])
                        em("vector", "tensor_scalar",
                           out=wQbs[slot][:, qs * 2048:(qs + 1) * 2048],
                           in0=i32scr[:], scalar1=-1.0, scalar2=1.0,
                           op0=ALU.max, op1=ALU.min, reads=[b("i32")],
                           writes=[b(f"wQb{slot}")])
                    dma("scalar", d_wgvq[c], wQbs[slot],
                        reads=[b(f"wQb{slot}")], writes=[b("d_wgvq")])
                else:
                    dma("sync", wQbs[slot], d_wgvq[c], reads=[b("d_wgvq")],
                        writes=[b(f"wQb{slot}")])

            wgv_stage(0)
            for c in range(64):
                slot = c % 2
                if c + 1 < 64:
                    wgv_stage(c + 1)
                wQbv = wQbs[slot]
                bwQb = b(f"wQb{slot}")
                for sl in range(2):
                    psA, bpA = [(ps_a, b("ps_a")), (ps_b, b("ps_b"))][sl]
                    psB, bpB = [(ps_c, b("ps_c")), (ps_d, b("ps_d"))][sl]
                    pairsA = [(wQbv[:, ki * 256:ki * 256 + 128],
                               hqT[:, ki * 1024 + sl * 512:
                                   ki * 1024 + sl * 512 + 512])
                              for ki in range(NKI)]
                    pairsB = [(wQbv[:, ki * 256 + 128:ki * 256 + 256],
                               hqT[:, ki * 1024 + sl * 512:
                                   ki * 1024 + sl * 512 + 512])
                              for ki in range(NKI)]
                    mm_group(psA[:], bpA, pairsA, [bwQb, b("hq")])
                    mm_group(psB[:], bpB, pairsB, [bwQb, b("hq")])
                    em("vector", "tensor_tensor", out=fu1[:, 0:512],
                       in0=psA[:], in1=dscT[:, sl * 512:(sl + 1) * 512],
                       op=ALU.mult, reads=[bpA, b("dsc")], writes=[b("fu1")])
                    em("vector", "tensor_tensor", out=fq1[:, 0:512],
                       in0=psB[:], in1=dscT[:, sl * 512:(sl + 1) * 512],
                       op=ALU.mult, reads=[bpB, b("dsc")], writes=[b("fq1")])
                    em("scalar", "activation", out=fu2[:, 0:512],
                       in_=fq1[:, 0:512], func=AF.Silu, reads=[b("fq1")],
                       writes=[b("fu2")])
                    uap = uslot[:, (c % 2) * 1024 + sl * 512:
                                (c % 2) * 1024 + sl * 512 + 512]
                    bu = b(f"uslot{c % 2}")
                    em("gpsimd", "tensor_tensor", out=uap, in0=fu1[:, 0:512],
                       in1=fu2[:, 0:512], op=ALU.mult,
                       reads=[b("fu1"), b("fu2")], writes=[bu])
                em("scalar", "activation", out=fq1[:],
                   in_=uslot[:, (c % 2) * 1024:(c % 2) * 1024 + 1024],
                   func=AF.Abs, reads=[b(f"uslot{c % 2}")], writes=[b("fq1")])
                src, bs = (umx, b("umx2")) if c % 2 == 0 else (scu, b("scu2"))
                dst, bd = (scu, b("scu2")) if c % 2 == 0 else (umx, b("umx2"))
                em("vector", "tensor_tensor", out=dst, in0=src, in1=fq1[:],
                   op=ALU.max, reads=[bs, b("fq1")], writes=[bd])
                dma("scalar", d_u[c][:, gbase:gbase + 1024],
                    uslot[:, (c % 2) * 1024:(c % 2) * 1024 + 1024],
                    reads=[b(f"uslot{c % 2}")], writes=[b("d_u")])
            tchain(umx, b("umx2"), umn, b("umn2"), scu, b("scu2"), invu,
                   b("invu2"))
            em("vector", "tensor_scalar", out=dscT, in0=invu,
               scalar1=sou_c[:, 1:2], scalar2=None, op0=ALU.mult,
               reads=[b("invu2"), b("sou")], writes=[b("dsc")])
            tap(f"hq{g}", R0[:, 0:NKI * 1024], b("hq"))
            tap(f"scuu{g}", scu[:], b("scu2"))
            tap(f"uall{g}", d_u, b("d_u"))
            P.barrier()
            dump("u" + str(g), [("scu_u", scu, b("scu2")),
                                ("dscu", dscT, b("dsc")),
                                ("u0d", d_u[0], b("d_u")),
                                ("dsch0", RA[:, 10240:11264], b("dsccp")),
                                ("u63d", d_u[63], b("d_u")),
                                ("hq", R0[:, 0:16384], b("hq")),
                                ("sgv", sgv_c[:], b("sgv"))])

            uslab = R0
            wObs = [RA[:, 0:8192], RA[:, 8192:16384]]
            for sl in range(2):
                for hc in range(64):
                    dma("sync", uslab[:, hc * 512:(hc + 1) * 512],
                        d_u[hc][:, gbase + sl * 512:gbase + sl * 512 + 512],
                        reads=[b("d_u")], writes=[b("uslab")])
                screp = RB[:, 0:2048]
                for rr in range(4):
                    em("vector", "tensor_copy",
                       out=screp[:, rr * 512:(rr + 1) * 512],
                       in_=scu[:, sl * 512:(sl + 1) * 512],
                       reads=[b("scu2")], writes=[b("screp")])
                for j in range(16):
                    em("vector", "tensor_tensor", out=RB[:, 4096:6144],
                       in0=uslab[:, j * 2048:(j + 1) * 2048],
                       in1=screp, op=ALU.mult,
                       reads=[b("uslab"), b("screp")], writes=[b("uq1")])
                    em("gpsimd", "tensor_copy", out=i32scr[:],
                       in_=RB[:, 4096:6144], reads=[b("uq1")],
                       writes=[b("i32")])
                    em("vector", "tensor_scalar",
                       out=uslab[:, j * 2048:(j + 1) * 2048],
                       in0=i32scr[:], scalar1=-128.0, scalar2=127.0,
                       op0=ALU.max, op1=ALU.min, reads=[b("i32")],
                       writes=[b("uslab")])
                def wout_stage(mo):
                    ws = mo % 2
                    if g == 0 and sl == 0:
                        wsrc = wout_b[mo]
                        stage = [RC, RD][ws]
                        extra = [b("rcs0"), b("rcs1")] if ws == 0 else \
                            [b("rd0"), b("rd1")]
                        dma("sync", stage[:, 0:4096],
                            wsrc[:, 0:4096], writes=[b(f"wO{ws}")] + extra)
                        for qs in range(2):
                            em("scalar", "activation", out=i32scr[:],
                               in_=stage[:, qs * 2048:(qs + 1) * 2048],
                               func=AF.Identity, scale=sou_c[:, 0:1],
                               bias=0.0, reads=[b(f"wO{ws}"), b("sou")],
                               writes=[b("i32")])
                            em("vector", "tensor_scalar",
                               out=wObs[ws][:, qs * 2048:(qs + 1) * 2048],
                               in0=i32scr[:], scalar1=-1.0, scalar2=1.0,
                               op0=ALU.max, op1=ALU.min, reads=[b("i32")],
                               writes=[b(f"wOb{ws}")])
                        dma("sync", stage[:, 0:4096],
                            wsrc[:, 4096:8192], writes=[b(f"wO{ws}")])
                        for qs in range(2):
                            em("scalar", "activation", out=i32scr[:],
                               in_=stage[:, qs * 2048:(qs + 1) * 2048],
                               func=AF.Identity, scale=sou_c[:, 0:1],
                               bias=0.0, reads=[b(f"wO{ws}"), b("sou")],
                               writes=[b("i32")])
                            em("vector", "tensor_scalar",
                               out=wObs[ws][:, 4096 + qs * 2048:
                                            4096 + (qs + 1) * 2048],
                               in0=i32scr[:], scalar1=-1.0, scalar2=1.0,
                               op0=ALU.max, op1=ALU.min, reads=[b("i32")],
                               writes=[b(f"wOb{ws}")])
                        dma("scalar", d_woutq[mo], wObs[ws],
                            reads=[b(f"wOb{ws}")], writes=[b("d_woutq")])
                    else:
                        dma("sync", wObs[ws], d_woutq[mo],
                            reads=[b("d_woutq")], writes=[b(f"wOb{ws}")])

                wout_stage(0)
                for mo in range(NKI):
                    if mo + 1 < NKI:
                        wout_stage(mo + 1)
                    wObv = wObs[mo % 2]
                    bwOb = b(f"wOb{mo % 2}")
                    pst, bp = [(ps_a, b("ps_a")), (ps_b, b("ps_b"))][mo % 2]
                    pairs = [(wObv[:, hc * 128:(hc + 1) * 128],
                              uslab[:, hc * 512:(hc + 1) * 512])
                             for hc in range(64)]
                    mm_group(pst[:], bp, pairs, [bwOb, b("uslab")])
                    em("vector", "tensor_tensor", out=fu2[:, 0:512],
                       in0=pst[:], in1=dscT[:, sl * 512:(sl + 1) * 512],
                       op=ALU.mult, reads=[bp, b("dsc")], writes=[b("of")])
                    dma("scalar",
                        d_outf[mo][:, gbase + sl * 512:gbase + sl * 512 + 512],
                        fu2[:, 0:512], reads=[b("of")], writes=[b("d_outf")])
            P.barrier()
            dump("f4" + str(g), [("dscu", dscT, b("dsc")),
                                 ("scu_u", scu, b("scu2")),
                                 ("du", d_u, b("d_u")),
                                 ("outf0", d_outf[0], b("d_outf")),
                                 ("outf1", d_outf[1], b("d_outf")),
                                 ("woq0", d_woutq[0], b("d_woutq")),
                                 ("woq1", d_woutq[1], b("d_woutq")),
                                 ("sou", sou_c[:], b("sou"))])

            tap(f"outf{g}", d_outf, b("d_outf"))
            tap(f"x2f{g}", d_x2, b("d_x2"))
            a2s = RAf[:, 0:512]
            b2s = RAf[:, 512:1024]
            a4s = RAf[:, 1024:1536]
            b4s = RAf[:, 1536:2048]
            s1a = RAf[:, 2048:2560]
            s1b = RAf[:, 2560:3072]
            s1c = RAf[:, 3072:3584]
            l5a = RAf[:, 3584:4096]
            l5b = RAf[:, 4096:4608]
            cast5 = RA[:, 10240:10752]
            sq5 = RA[:, 10752:11264]
            for ss in range(2):
                t0 = gbase + ss * 512
                o1 = o2 = None
                for ki in range(NKI):
                    dma("sync", RB[:, ki * 512:(ki + 1) * 512],
                        d_x2[ki][:, t0:t0 + 512], reads=[b("d_x2")],
                        writes=[b("x2s")])
                    dma("scalar", RC[:, ki * 512:(ki + 1) * 512],
                        d_outf[ki][:, t0:t0 + 512], reads=[b("d_outf")],
                        writes=[b("ofs")])
                    em("gpsimd", "tensor_copy", out=cast5,
                       in_=RB[:, ki * 512:(ki + 1) * 512], reads=[b("x2s")],
                       writes=[b("cast5")])
                    em("scalar", "activation", out=sq5,
                       in_=RB[:, ki * 512:(ki + 1) * 512], func=AF.Square,
                       reads=[b("x2s")], writes=[b("sq5")])
                    kw = dict(start=(ki == 0), stop=(ki == NKI - 1))
                    o1 = em("tensor", "matmul", out=ps_c[:], lhsT=ones_bf[:],
                            rhs=cast5, reads=[b("ones_bf"), b("cast5")],
                            writes=[b("ps_c")] if ki == 0 else [], **kw)
                    o2 = em("tensor", "matmul", out=ps_d[:], lhsT=ones_bf[:],
                            rhs=sq5, reads=[b("ones_bf"), b("sq5")],
                            writes=[b("ps_d")] if ki == 0 else [], **kw)
                if o1 is not None:
                    b("ps_c").writer = o1; b("ps_c").readers = []
                    b("ps_d").writer = o2; b("ps_d").readers = []
                em("vector", "tensor_copy", out=l5a, in_=ps_c[:],
                   reads=[b("ps_c")], writes=[b("l5a")])
                em("vector", "tensor_copy", out=l5b, in_=ps_d[:],
                   reads=[b("ps_d")], writes=[b("l5b")])
                ln_stats_combine(a2s, b("a2s"), b2s, b("b2s"), l5a, b("l5a"),
                                 l5b, b("l5b"), s1a, b("s1a"), s1b, b("s1b"))
                o1 = o2 = None
                for ki in range(NKI):
                    em("vector", "tensor_tensor", out=s1a,
                       in0=RB[:, ki * 512:(ki + 1) * 512], in1=a2s,
                       op=ALU.mult, reads=[b("x2s"), b("a2s")],
                       writes=[b("s1a")])
                    em("gpsimd", "tensor_tensor", out=s1b, in0=s1a, in1=b2s,
                       op=ALU.add, reads=[b("s1a"), b("b2s")],
                       writes=[b("s1b")])
                    em("vector", "tensor_tensor", out=s1c, in0=s1b,
                       in1=RC[:, ki * 512:(ki + 1) * 512], op=ALU.add,
                       reads=[b("s1b"), b("ofs")], writes=[b("s1c")])
                    em("gpsimd", "tensor_copy", out=cast5, in_=s1c,
                       reads=[b("s1c")], writes=[b("cast5")])
                    em("scalar", "activation", out=sq5, in_=s1c,
                       func=AF.Square, reads=[b("s1c")], writes=[b("sq5")])
                    kw = dict(start=(ki == 0), stop=(ki == NKI - 1))
                    o1 = em("tensor", "matmul", out=ps_f[:], lhsT=ones_bf[:],
                            rhs=cast5, reads=[b("ones_bf"), b("cast5")],
                            writes=[b("ps_f")] if ki == 0 else [], **kw)
                    o2 = em("tensor", "matmul", out=ps_h[:], lhsT=ones_bf[:],
                            rhs=sq5, reads=[b("ones_bf"), b("sq5")],
                            writes=[b("ps_h")] if ki == 0 else [], **kw)
                if o1 is not None:
                    b("ps_f").writer = o1; b("ps_f").readers = []
                    b("ps_h").writer = o2; b("ps_h").readers = []
                em("vector", "tensor_copy", out=l5a, in_=ps_f[:],
                   reads=[b("ps_f")], writes=[b("l5a")])
                em("vector", "tensor_copy", out=l5b, in_=ps_h[:],
                   reads=[b("ps_h")], writes=[b("l5b")])
                ln_stats_combine(a4s, b("a4s"), b4s, b("b4s"), l5a, b("l5a"),
                                 l5b, b("l5b"), s1a, b("s1a"), s1b, b("s1b"))
                for ki in range(NKI):
                    em("vector", "tensor_tensor", out=s1a,
                       in0=RB[:, ki * 512:(ki + 1) * 512], in1=a2s,
                       op=ALU.mult, reads=[b("x2s"), b("a2s")],
                       writes=[b("s1a")])
                    em("gpsimd", "tensor_tensor", out=s1b, in0=s1a, in1=b2s,
                       op=ALU.add, reads=[b("s1a"), b("b2s")],
                       writes=[b("s1b")])
                    em("vector", "tensor_tensor", out=s1c, in0=s1b,
                       in1=RC[:, ki * 512:(ki + 1) * 512], op=ALU.add,
                       reads=[b("s1b"), b("ofs")], writes=[b("s1c")])
                    em("vector", "tensor_tensor", out=s1a, in0=s1c, in1=a4s,
                       op=ALU.mult, reads=[b("s1c"), b("a4s")],
                       writes=[b("s1a")])
                    em("gpsimd", "tensor_tensor", out=s1b, in0=s1a, in1=b4s,
                       op=ALU.add, reads=[b("s1a"), b("b4s")],
                       writes=[b("s1b")])
                    em("vector", "tensor_tensor", out=s1c, in0=s1b,
                       in1=RB[:, ki * 512:(ki + 1) * 512], op=ALU.add,
                       reads=[b("s1b"), b("x2s")], writes=[b("s1c")])
                    dma("sync", y[ki][:, t0:t0 + 512], s1c,
                        reads=[b("s1c")], writes=[b("y")])
            P.barrier()
            dump("f5" + str(g), [("x2f", d_x2, b("d_x2")),
                                 ("outff", d_outf, b("d_outf")),
                                 ("y0", y[0], b("y")),
                                 ("y5", y[5], b("y"))])

        em("sync", "_noop", waits=[x for x in [b("y").writer] if x]
           or [P.last_real("sync")])

    return nc, P, dbg


def _finish(nc, P):
    sems = {}
    with ExitStack() as stack:
        E = stack.enter_context
        for e in ENGINES:
            sems[(e, False)] = E(nc.semaphore(f"s_{e}"))
        for e in ("sync", "scalar", "gpsimd", "vector"):
            sems[(e, True)] = E(nc.semaphore(f"d_{e}"))
        blk = E(nc.Block())
        blk.tensor(lambda eng: P.replay(eng, "tensor", sems))
        blk.vector(lambda eng: P.replay(eng, "vector", sems))
        blk.scalar(lambda eng: P.replay(eng, "scalar", sems))
        blk.gpsimd(lambda eng: P.replay(eng, "gpsimd", sems))
        blk.sync(lambda eng: P.replay(eng, "sync", sems))


_CACHED = {}


def _get_program():
    if "nc" not in _CACHED:
        nc, P, dbg = build_nc()
        P.finalize()
        _finish(nc, P)
        _CACHED.update(nc=nc, P=P, dbg=dbg)
    return _CACHED["nc"], _CACHED["P"], _CACHED["dbg"]


def kernel(L0_attn_x, L0_attn_wq, L0_attn_bq, L0_attn_wk, L0_attn_bk,
           L0_attn_wv, L0_attn_bv, L0_attn_wg, L0_attn_bg, L0_attn_wo,
           L0_attn_bo, L0_attn_gn_w, L0_attn_gn_b, L0_attn_gamma,
           L0_ffn_w_norm, L0_ffn_w_gv, L0_ffn_w_out):
    x = np.ascontiguousarray(np.asarray(L0_attn_x, np.float32))
    nc, P, dbg = _get_program()

    def blocked(w):
        return np.ascontiguousarray(
            np.asarray(w, np.float32).reshape(NKI, 128, H, 128)
            .transpose(2, 1, 0, 3).reshape(H, 128, NKI * 128))

    wq_bv = blocked(L0_attn_wq)
    wk_bv = blocked(L0_attn_wk)
    wv_bv = blocked(L0_attn_wv)
    wg_bv = blocked(L0_attn_wg)
    wo = np.ascontiguousarray(np.asarray(L0_attn_wo, np.float32))
    vecs = np.ascontiguousarray(
        np.stack([L0_attn_bq, L0_attn_bk, L0_attn_bv, L0_attn_bg, L0_attn_bo,
                  L0_attn_gn_w, L0_attn_gn_b, L0_ffn_w_norm])
        .astype(np.float32).reshape(128, 128))
    gammaP = np.ascontiguousarray(
        np.asarray(L0_attn_gamma, np.float32).reshape(H, 1))
    A = np.asarray(L0_ffn_w_gv, np.float32).T
    val = A[:, :FFH].reshape(NKI, 128, 64, 128)
    gat = A[:, FFH:].reshape(NKI, 128, 64, 128)
    pair = np.concatenate([val, gat], axis=3)
    wgv_bv = np.ascontiguousarray(
        pair.transpose(2, 1, 0, 3).reshape(64, 128, NKI * 256))
    WT = np.asarray(L0_ffn_w_out, np.float32).T
    wout_bv = np.ascontiguousarray(
        WT.reshape(64, 128, NKI, 128).transpose(2, 1, 0, 3)
        .reshape(NKI, 128, 64 * 128))

    in_maps = []
    for c in range(8):
        bb = c >> 1
        own = c & 1
        hm = np.zeros((128, 2), np.float32)
        hm[:, 0] = 0.0 if own else 1.0
        hm[:, 1] = 1.0 if own else 0.0
        in_maps.append(dict(
            xT=np.ascontiguousarray(x[bb].T),
            xownT=np.ascontiguousarray(x[bb, own * TOK:(own + 1) * TOK].T),
            wq_b=wq_bv, wk_b=wk_bv, wv_b=wv_bv, wg_b=wg_bv, wo=wo,
            vecs=vecs,
            gammaP=gammaP, wgv_b=wgv_bv, wout_b=wout_bv, hmask=hm))

    res = run_bass_kernel_spmd(nc, in_maps, core_ids=list(range(8)))

    if dbg["outs"] or (dbg.get("taps") and os.environ.get("KTAPRET")):
        return res

    out = np.empty((4, S, D), np.float32)
    for c in range(8):
        bb = c >> 1
        own = c & 1
        yv = res.results[c]["y"]
        out[bb, own * TOK:(own + 1) * TOK, :] = (
            np.asarray(yv).transpose(2, 0, 1).reshape(TOK, D))
    return out
